# version 03
# baseline (speedup 1.0000x reference)
import contextlib
import numpy as np
import ml_dtypes
import concourse.bass as bass
import concourse.mybir as mybir
from concourse.bass_utils import run_bass_kernel_spmd

F32, BF16 = mybir.dt.float32, mybir.dt.bfloat16
AF = mybir.ActivationFunctionType
ALU = mybir.AluOpType
AX = mybir.AxisListType

T, D, E, DI, NST, RNK = 4096, 1024, 4096, 2048, 16, 64
TC = 512
NCH = T // TC
NDT = DI // 128
NEc = E // 128

_ACT_TABLES = None


def _table_id(name):
    global _ACT_TABLES
    if _ACT_TABLES is None:
        from concourse.hw_specs import get_activation_tables
        _ACT_TABLES = list(get_activation_tables("gen3").keys())
    return _ACT_TABLES.index(name)


def build():
    nc = bass.Bass("TRN2", target_bir_lowering=False, detect_race_conditions=False)

    x_in = nc.dram_tensor("x_in", [T, D], F32, kind="ExternalInput").ap()
    out_hbm = nc.dram_tensor("y_out", [T, D], F32, kind="ExternalOutput").ap()
    winT_st = nc.dram_tensor("winT_st", [8, 8, 128, 512], F32, kind="ExternalInput").ap()
    woutT_st = nc.dram_tensor("woutT_st", [4, 4, 128, 1024], F32, kind="ExternalInput").ap()
    w1T_l = nc.dram_tensor("w1T_l", [8, 128, 4096], F32, kind="ExternalInput").ap()
    w2T_l = nc.dram_tensor("w2T_l", [32, 128, 1024], F32, kind="ExternalInput").ap()
    w1n_l = nc.dram_tensor("w1n_l", [32, 128, 1024], F32, kind="ExternalInput").ap()
    w2n_l = nc.dram_tensor("w2n_l", [8, 128, 4096], F32, kind="ExternalInput").ap()
    xprojT_l = nc.dram_tensor("xprojT_l", [16, 128, 96], F32, kind="ExternalInput").ap()
    dtT_l = nc.dram_tensor("dtT_l", [64, 2048], F32, kind="ExternalInput").ap()
    alog_l = nc.dram_tensor("alog_l", [128, 256], F32, kind="ExternalInput").ap()
    cw_l = nc.dram_tensor("cw_l", [128, 64], F32, kind="ExternalInput").ap()
    cb_l = nc.dram_tensor("cb_l", [128, 16], F32, kind="ExternalInput").ap()
    dD_l = nc.dram_tensor("dD_l", [128, 16], F32, kind="ExternalInput").ap()
    dtb_l = nc.dram_tensor("dtb_l", [128, 16], F32, kind="ExternalInput").ap()
    b1_l = nc.dram_tensor("b1_l", [128, 32], F32, kind="ExternalInput").ap()
    b2_l = nc.dram_tensor("b2_l", [128, 8], F32, kind="ExternalInput").ap()
    idf_in = nc.dram_tensor("idf_in", [128, 128], F32, kind="ExternalInput").ap()
    idb_in = nc.dram_tensor("idb_in", [128, 128], BF16, kind="ExternalInput").ap()
    w1scr = nc.dram_tensor("w1scr", [8, 8, 128, 512], BF16, kind="Internal").ap()
    w2scr = nc.dram_tensor("w2scr", [8, 4, 128, 1024], BF16, kind="Internal").ap()
    d_bcflat = nc.dram_tensor("d_bcflat", [1, 32 * TC], BF16, kind="Internal").ap()

    with contextlib.ExitStack() as st:
        En = st.enter_context
        wbuf = [En(nc.sbuf_tensor(f"wbuf{i}", [128, 4096], BF16)) for i in range(2)]
        xprojT_sb = En(nc.sbuf_tensor("xprojT_sb", [128, 16 * 96], BF16))
        dtT_sb = En(nc.sbuf_tensor("dtT_sb", [64, 2048], BF16))
        A_sb = En(nc.sbuf_tensor("A_sb", [128, 256], F32))
        cw_sb = En(nc.sbuf_tensor("cw_sb", [128, 64], F32))
        cb_sb = En(nc.sbuf_tensor("cb_sb", [128, 16], F32))
        dD_sb = En(nc.sbuf_tensor("dD_sb", [128, 16], F32))
        dtb_sb = En(nc.sbuf_tensor("dtb_sb", [128, 16], F32))
        b1_sb = En(nc.sbuf_tensor("b1_sb", [128, 32], F32))
        b2_sb = En(nc.sbuf_tensor("b2_sb", [128, 8], F32))
        a1raw = En(nc.sbuf_tensor("a1raw", [128, 32], F32))
        a2raw = En(nc.sbuf_tensor("a2raw", [128, 8], F32))
        a1_sb = En(nc.sbuf_tensor("a1_sb", [128, 32], F32))
        a2_sb = En(nc.sbuf_tensor("a2_sb", [128, 8], F32))
        idf_sb = En(nc.sbuf_tensor("idf_sb", [128, 128], F32))
        idb_sb = En(nc.sbuf_tensor("idb_sb", [128, 128], BF16))
        tmA = En(nc.sbuf_tensor("tmA", [128, 4096], F32))
        tmB = En(nc.sbuf_tensor("tmB", [128, 4096], F32))
        xfm_bf = En(nc.sbuf_tensor("xfm_bf", [128, 8 * TC], BF16))
        ascr_bf = En(nc.sbuf_tensor("ascr_bf", [128, 4096], BF16))
        xs_bf = En(nc.sbuf_tensor("xs_bf", [128, NDT * TC], BF16))
        sz_bf = En(nc.sbuf_tensor("sz_bf", [128, NDT * TC], BF16))
        dt_bf = En(nc.sbuf_tensor("dt_bf", [128, NDT * TC], BF16))
        bc_sb = En(nc.sbuf_tensor("bc_sb", [128, 32 * TC], BF16))
        dtr_bf = En(nc.sbuf_tensor("dtr_bf", [64, TC], BF16))
        bcrows_bf = En(nc.sbuf_tensor("bcrows_bf", [32, TC], BF16))
        dtx_bf = En(nc.sbuf_tensor("dtx_bf", [128, 2 * TC], BF16))
        hu_bf = En(nc.sbuf_tensor("hu_bf", [128, 2 * TC], BF16))
        hc_bf = En(nc.sbuf_tensor("hc_bf", [128, 3 * TC], BF16))
        hst = En(nc.sbuf_tensor("hst", [128, 512], F32))
        halo = En(nc.sbuf_tensor("halo", [128, 96], F32))
        ygh_bf = En(nc.sbuf_tensor("ygh_bf", [128, NDT * TC], BF16))
        sc1 = En(nc.sbuf_tensor("sc1", [128, 1040], F32))
        sc2 = En(nc.sbuf_tensor("sc2", [128, 2048], F32))
        sc3 = En(nc.sbuf_tensor("sc3", [128, 1024], F32))
        lncol = En(nc.sbuf_tensor("lncol", [128, 32], F32))

        P1 = En(nc.psum_tensor("P1", [128, 512], F32))
        P2 = En(nc.psum_tensor("P2", [128, 512], F32))
        P3 = En(nc.psum_tensor("P3", [128, 512], F32))
        P4 = En(nc.psum_tensor("P4", [128, 512], F32))
        PF = En(nc.psum_tensor("PF", [128, 2048], F32))
        PA = [P1, P2]
        P34 = [P3, P4]
        PFb = PF[:].bitcast(BF16)

        names = ["qx", "qo", "qb", "qg0", "qg1", "qs0", "qs1", "qp", "qpg", "qc", "t", "v", "s"]
        SEM = {n: En(nc.semaphore(n)) for n in names}
        C = {n: 0 for n in names}

        blk = En(nc.Block())

        def inc(instr, sem, n=1):
            instr.then_inc(SEM[sem], n)
            C[sem] += n
            return C[sem]

        def w(eng, sem, val):
            if val > 0:
                eng.wait_ge(SEM[sem], val)

        def snap():
            return (C["t"], C["v"], C["s"])

        def TB(f, sn=None):
            sn = sn or snap()

            def g(te):
                w(te, "t", sn[0]); w(te, "v", sn[1]); w(te, "s", sn[2])
                f(te)
            blk.tensor(g)

        def VB(f, sn=None):
            sn = sn or snap()

            def g(v):
                w(v, "t", sn[0]); w(v, "v", sn[1]); w(v, "s", sn[2])
                f(v)
            blk.vector(g)

        def SB(f, sn=None):
            sn = sn or snap()

            def g(sc):
                w(sc, "t", sn[0]); w(sc, "v", sn[1]); w(sc, "s", sn[2])
                f(sc)
            blk.scalar(g)

        pa_hist = []

        def pa_wait(te):
            if len(pa_hist) >= 2:
                sem, val = pa_hist[-2]
                w(te, sem, val)

        unit_state = {"gu": 0, "emitted": 0}
        unit_info = {}
        unit_tval = {}
        unit_wait = {}
        ucount = {"qg0": 0, "qg1": 0, "qs0": 0, "qs1": 0}

        def unit_wait_of(gu):
            return unit_wait[gu]

        def queue_unit(eng_name, src_ap, kgroups):
            gu = unit_state["gu"]
            unit_state["gu"] += 1
            semname = ("qg" if eng_name == "gpsimd" else "qs") + str(gu % 2)
            ucount[semname] += 16
            unit_wait[gu] = (semname, ucount[semname])
            unit_info[gu] = (eng_name, src_ap, kgroups)
            return gu

        def flush_units():
            while unit_state["emitted"] < unit_state["gu"]:
                gu = unit_state["emitted"]
                if gu >= 2 and (gu - 2) not in unit_tval:
                    return
                eng_name, src_ap, kg = unit_info[gu]
                tv = unit_tval.get(gu - 2, 0)

                def body(eng, gu=gu, src_ap=src_ap, kg=kg, tv=tv):
                    w(eng, "t", tv)
                    ins = eng.dma_start(
                        out=wbuf[gu % 2][:].rearrange("p (k j) -> p k j", k=kg),
                        in_=src_ap)
                    semname, val = unit_wait[gu]
                    inc(ins, semname, 16)
                    assert C[semname] == val
                getattr(blk, eng_name)(body)
                unit_state["emitted"] += 1

        def prep_sync1(sync):
            inc(sync.dma_start(out=sc3[:, 0:256], in_=alog_l), "qp", 16)
            inc(sync.dma_start(out=cw_sb[:], in_=cw_l), "qp", 16)
            inc(sync.dma_start(out=cb_sb[:], in_=cb_l), "qp", 16)
            inc(sync.dma_start(out=dD_sb[:], in_=dD_l), "qp", 16)
            inc(sync.dma_start(out=dtb_sb[:], in_=dtb_l), "qp", 16)
            inc(sync.dma_start(out=b1_sb[:], in_=b1_l), "qp", 16)
            inc(sync.dma_start(out=b2_sb[:], in_=b2_l), "qp", 16)
            inc(sync.dma_start(out=idf_sb[:], in_=idf_in), "qp", 16)
            inc(sync.dma_start(out=idb_sb[:], in_=idb_in), "qp", 16)
        blk.sync(prep_sync1)

        def prep_gp1(g):
            inc(g.dma_start(out=xprojT_sb[:].rearrange("p (k j) -> p k j", k=16),
                            in_=xprojT_l.transpose([1, 0, 2])), "qpg", 16)
            inc(g.dma_start(out=dtT_sb[:], in_=dtT_l), "qpg", 16)
        blk.gpsimd(prep_gp1)
        qp_small = C["qp"]

        def prep_v0(v):
            inc(v.memset(hst[:], 0.0), "v")
            inc(v.memset(lncol[:, 28:29], 1e-5), "v")
            inc(v.memset(halo[:], 0.0), "v")
        VB(prep_v0)

        def prep_s1(s):
            s.add_instruction(mybir.InstLoadActFuncSet(
                name=nc.get_next_instruction_name(),
                act_func_set_id=_table_id("natural_log_exp_and_others"), ins=[], outs=[]))
            w(s, "qp", qp_small)
            inc(s.activation(out=sc3[:, 256:512], in_=sc3[:, 0:256], func=AF.Exp), "s")
        SB(prep_s1)
        s_aexp = C["s"]

        def prep_v1(v):
            w(v, "s", s_aexp)
            inc(v.tensor_scalar(A_sb[:], sc3[:, 256:512], -1.0, None, ALU.mult), "v")
        VB(prep_v1)

        for g in range(16):
            vmark = C["v"]

            def al_sync(sync, g=g, vmark=vmark):
                w(sync, "v", vmark)
                if g < 8:
                    ins = sync.dma_start(out=tmA[:].rearrange("p (u j) -> p u j", u=4),
                                         in_=w1n_l[4 * g:4 * g + 4].transpose([1, 0, 2]))
                else:
                    ins = sync.dma_start(out=tmA[:], in_=w2n_l[g - 8])
                inc(ins, "qc", 16)
            blk.sync(al_sync)
            qpm = C["qc"]

            def al_v(v, g=g, qpm=qpm):
                w(v, "qc", qpm)
                if g < 8:
                    for u in range(4):
                        inc(v.tensor_reduce(a1raw[:, 4 * g + u:4 * g + u + 1],
                                            tmA[:, u * 1024:(u + 1) * 1024], AX.X, ALU.add,
                                            apply_absolute_value=True), "v")
                else:
                    inc(v.tensor_reduce(a2raw[:, g - 8:g - 7], tmA[:], AX.X, ALU.add,
                                        apply_absolute_value=True), "v")
            VB(al_v)

        def prep_v2(v):
            inc(v.tensor_scalar(a1_sb[:], a1raw[:], 1.0 / 1024, None, ALU.mult), "v")
            inc(v.tensor_scalar(a2_sb[:], a2raw[:], 1.0 / 4096, None, ALU.mult), "v")
        VB(prep_v2)
        v_alpha = C["v"]

        for k in range(16):
            smark = C["s"]
            qpr = C["qc"]

            def sg_sync(sync, k=k, smark=smark, qpr=qpr):
                w(sync, "s", smark)
                w(sync, "qc", qpr)
                if k == 0:
                    w(sync, "v", v_alpha)
                if k < 8:
                    ins = sync.dma_start(out=tmA[:], in_=w1T_l[k])
                else:
                    g2 = k - 8
                    ins = sync.dma_start(out=tmA[:].rearrange("p (u j) -> p u j", u=4),
                                         in_=w2T_l[4 * g2:4 * g2 + 4].transpose([1, 0, 2]))
                inc(ins, "qc", 16)
            blk.sync(sg_sync)
            qpm = C["qc"]

            def sg_s(s, qpm=qpm):
                w(s, "qc", qpm)
                inc(s.activation(out=ascr_bf[:], in_=tmA[:], func=AF.Sign), "s")
            SB(sg_s)
            smark2 = C["s"]

            def sg_sync2(sync, k=k, smark2=smark2):
                w(sync, "s", smark2)
                if k < 8:
                    ins = sync.dma_start(out=w1scr[:, k].transpose([1, 0, 2]),
                                         in_=ascr_bf[:].rearrange("p (q j) -> p q j", q=8))
                else:
                    g2 = k - 8
                    ins = sync.dma_start(out=w2scr[g2].transpose([1, 0, 2]),
                                         in_=ascr_bf[:].rearrange("p (u j) -> p u j", u=4))
                inc(ins, "qc", 16)
            blk.sync(sg_sync2)

        s_signdone = C["s"]
        qc_final = C["qc"]

        def x0_sync(sync):
            w(sync, "s", s_signdone)
            inc(sync.dma_start(out=tmA[:].rearrange("p (s d) -> p s d", s=4),
                               in_=x_in[0:TC, :].rearrange("(s p) d -> p s d", p=128)), "qx", 16)
        blk.sync(x0_sync)

        marks = {"ffnT": [0] * NCH}

        for c in range(NCH):
            cp = c % 2
            ubase = unit_state["gu"]
            for q in range(8):
                queue_unit("gpsimd", winT_st[q].transpose([1, 0, 2]), 8)
            for h in range(4):
                queue_unit("gpsimd", woutT_st[h].transpose([1, 0, 2]), 4)
            for th in range(2):
                for q in range(8):
                    queue_unit("sync", w1scr[q].transpose([1, 0, 2]), 8)
                for g in range(8):
                    queue_unit("sync", w2scr[g].transpose([1, 0, 2]), 4)
            flush_units()

            if c > 0:
                def ph0(sync, c=c):
                    w(sync, "qo", 16 * c)
                    w(sync, "t", marks["ffnT"][c - 1])
                    inc(sync.dma_start(
                        out=tmA[:].rearrange("p (s d) -> p s d", s=4),
                        in_=x_in[c * TC:(c + 1) * TC, :].rearrange("(s p) d -> p s d", p=128)),
                        "qx", 16)
                blk.sync(ph0)

            sn = snap()
            v0 = C["v"]
            t0 = C["t"]

            def ph1_t(te, c=c, v0=v0):
                w(te, "qx", 16 * (c + 1))
                for j in range(8):
                    pa_wait(te)
                    for s4 in range(4):
                        te.transpose(PA[j % 2][:, s4 * 128:(s4 + 1) * 128],
                                     tmA[:, s4 * 1024 + j * 128:s4 * 1024 + (j + 1) * 128],
                                     idf_sb[:])
                    inc(te.matmul(PF[:, 0:64], idb_sb[:], idb_sb[:, 0:64],
                                  start=True, stop=True), "t")
                    pa_hist.append(("v", v0 + j + 1))
            TB(ph1_t, sn)
            flush_units()

            def ph1_v(v, c=c, t0=t0):
                if c > 0:
                    w(v, "t", marks["ffnT"][c - 1])
                for j in range(8):
                    w(v, "t", t0 + j + 1)
                    inc(v.tensor_copy(xfm_bf[:, j * TC:(j + 1) * TC], PA[j % 2][:]), "v")
            VB(ph1_v, sn)
            assert C["t"] == t0 + 8 and C["v"] == v0 + 8
            xfm_ready_v = C["v"]

            sn = snap()
            t0, v0, s0 = C["t"], C["v"], C["s"]

            def conv_tap3_val(i):
                return v0 + 6 * i + 5

            def conv_xzfree_val(i):
                return v0 + 6 * i + 6

            ph2_order = []
            for i in range(NEc):
                ph2_order.append(("evac", i))
                if 2 <= i < 18:
                    ph2_order.append(("silu", i - 2))
            evac_mark = {}
            silu_mark = {}
            for pos, (kind, i) in enumerate(ph2_order):
                (evac_mark if kind == "evac" else silu_mark)[i] = s0 + pos + 1


            def ph2_t(te, c=c, s0=s0, ubase=ubase):
                w(te, "v", xfm_ready_v)
                for i in range(NEc):
                    q = i // 4
                    if i % 4 == 0:
                        sem, val = unit_wait_of(ubase + q)
                        w(te, sem, val)
                    pa_wait(te)
                    bi = (ubase + q) % 2
                    col0 = (i % 4) * 128
                    for k in range(8):
                        ins = te.matmul(PA[i % 2][:],
                                        wbuf[bi][:, k * 512 + col0:k * 512 + col0 + 128],
                                        xfm_bf[:, k * TC:(k + 1) * TC],
                                        start=(k == 0), stop=(k == 7))
                    inc(ins, "t")
                    if i % 4 == 3:
                        unit_tval[ubase + q] = C["t"]
                    pa_hist.append(("s", evac_mark[i]))
            TB(ph2_t, sn)
            flush_units()

            def ph2_s(s, c=c, t0=t0, v0=v0):
                s.add_instruction(mybir.InstLoadActFuncSet(
                    name=nc.get_next_instruction_name(),
                    act_func_set_id=_table_id("silu_and_others"), ins=[], outs=[]))
                for kind, i in ph2_order:
                    if kind == "evac":
                        w(s, "t", t0 + i + 1)
                        if i < 16:
                            if i >= 2:
                                w(s, "v", conv_xzfree_val(i - 2))
                            sl = (i % 2) * 520
                            inc(s.copy(sc1[:, sl + 3:sl + 515], PA[i % 2][:]), "s")
                        else:
                            inc(s.activation(out=sz_bf[:, (i - 16) * TC:(i - 15) * TC],
                                             in_=PA[i % 2][:], func=AF.Silu), "s")
                    else:
                        w(s, "v", conv_tap3_val(i))
                        bslot = 1024 + (i % 2) * 512
                        inc(s.activation(out=xs_bf[:, i * TC:(i + 1) * TC],
                                         in_=sc2[:, bslot:bslot + 512], func=AF.Silu,
                                         bias=cb_sb[:, i:i + 1]), "s")
            SB(ph2_s, sn)

            def ph2_v(v, c=c, s0=s0, cp=cp):
                for i in range(16):
                    w(v, "s", evac_mark[i])
                    w(v, "v", C["v"])
                    xz = sc1[:, (i % 2) * 520:(i % 2) * 520 + 520]
                    aslot = (i % 2) * 512
                    bslot = 1024 + (i % 2) * 512
                    inc(v.tensor_copy(xz[:, 0:3], halo[:, cp * 48 + i * 3:cp * 48 + i * 3 + 3]), "v")
                    inc(v.tensor_scalar(sc2[:, aslot:aslot + 512], xz[:, 0:512],
                                        cw_sb[:, 4 * i:4 * i + 1], None, ALU.mult), "v")
                    if i >= 2:
                        w(v, "s", silu_mark[i - 2])
                    inc(v.scalar_tensor_tensor(sc2[:, bslot:bslot + 512], xz[:, 1:513],
                                               cw_sb[:, 4 * i + 1:4 * i + 2],
                                               sc2[:, aslot:aslot + 512], ALU.mult, ALU.add), "v")
                    w(v, "v", C["v"])
                    inc(v.scalar_tensor_tensor(sc2[:, aslot:aslot + 512], xz[:, 2:514],
                                               cw_sb[:, 4 * i + 2:4 * i + 3],
                                               sc2[:, bslot:bslot + 512], ALU.mult, ALU.add), "v")
                    w(v, "v", C["v"])
                    inc(v.scalar_tensor_tensor(sc2[:, bslot:bslot + 512], xz[:, 3:515],
                                               cw_sb[:, 4 * i + 3:4 * i + 4],
                                               sc2[:, aslot:aslot + 512], ALU.mult, ALU.add), "v")
                    inc(v.tensor_copy(halo[:, (1 - cp) * 48 + i * 3:(1 - cp) * 48 + i * 3 + 3],
                                      xz[:, 512:515]), "v")
            VB(ph2_v, sn)
            assert C["v"] == v0 + 96
            assert C["s"] == s0 + 48

            sn = snap()

            def ph3_t1(te, silu_mark=silu_mark):
                w(te, "qpg", 32)
                for k in range(16):
                    w(te, "s", silu_mark[k])
                    ins = te.matmul(P3[0:96, :], xprojT_sb[:, k * 96:(k + 1) * 96],
                                    xs_bf[:, k * TC:(k + 1) * TC],
                                    start=(k == 0), stop=(k == 15))
                inc(ins, "t")
            TB(ph3_t1, sn)
            t_xproj = C["t"]

            def ph3_v(v, t_xproj=t_xproj):
                w(v, "t", t_xproj)
                inc(v.tensor_copy(dtr_bf[:], P3[0:64, :]), "v")
                inc(v.tensor_copy(bcrows_bf[:], P3[64:96, :]), "v")
            VB(ph3_v, sn)
            v_bcrows = C["v"]

            def ph3_sync(sync, v_bcrows=v_bcrows):
                w(sync, "v", v_bcrows)
                inc(sync.dma_start(out=d_bcflat.rearrange("o (r t) -> r (o t)", r=32),
                                   in_=bcrows_bf[:]), "qb", 16)
                w(sync, "qb", C["qb"])
                inc(sync.dma_start(out=bc_sb[:], in_=d_bcflat.partition_broadcast(128)), "qb", 16)
            blk.sync(ph3_sync)

            sn = snap()
            s0_ph3 = C["s"]
            t0 = C["t"]

            def softp_val(k):
                return s0_ph3 + 2 * k + 2

            def ph3_t2(te, s0_ph3=s0_ph3, v_bcrows=v_bcrows):
                w(te, "v", v_bcrows)
                for k in range(16):
                    pa_wait(te)
                    inc(te.matmul(PA[k % 2][:], dtT_sb[:, k * 128:(k + 1) * 128],
                                  dtr_bf[:], start=True, stop=True), "t")
                    pa_hist.append(("s", softp_val(k)))
            TB(ph3_t2, sn)
            flush_units()

            def ph3_s(s, t0=t0):
                s.add_instruction(mybir.InstLoadActFuncSet(
                    name=nc.get_next_instruction_name(),
                    act_func_set_id=_table_id("natural_log_exp_and_others"), ins=[], outs=[]))
                for k in range(16):
                    w(s, "s", C["s"])
                    w(s, "t", t0 + k + 1)
                    inc(s.activation(out=sc3[:, 0:512], in_=PA[k % 2][:], func=AF.Exp,
                                     bias=dtb_sb[:, k:k + 1]), "s")
                    inc(s.activation(out=dt_bf[:, k * TC:(k + 1) * TC], in_=sc3[:, 0:512],
                                     func=AF.Ln, bias=1.0), "s")
            SB(ph3_s, sn)

            sn = snap()
            s0, v0, t0 = C["s"], C["v"], C["t"]

            def dA_val(j):
                return s0 + j + 1

            def hC_val(j):
                return v0 + 67 * (j // 16) + 4 * (j % 16) + 5

            def gate_val(dt_i):
                return v0 + 67 * dt_i + 67

            def idmm_val(j):
                return t0 + j + 1

            def ph4_s(s):
                for j in range(256):
                    dt_i, n = j // 16, j % 16
                    if j >= 2:
                        w(s, "v", hC_val(j - 2))
                    inc(s.activation(out=sc1[:, (j % 2) * 520:(j % 2) * 520 + 512],
                                     in_=dt_bf[:, dt_i * TC:(dt_i + 1) * TC], func=AF.Exp,
                                     scale=A_sb[:, dt_i * 16 + n:dt_i * 16 + n + 1]), "s")
            SB(ph4_s, sn)

            def ph4_v(v, c=c, cp=cp):
                w(v, "qb", 32 * (c + 1))
                for dt_i in range(NDT):
                    w(v, "s", softp_val(dt_i))
                    w(v, "v", C["v"])
                    inc(v.tensor_tensor(dtx_bf[:, (dt_i % 2) * TC:(dt_i % 2 + 1) * TC],
                                        dt_bf[:, dt_i * TC:(dt_i + 1) * TC],
                                        xs_bf[:, dt_i * TC:(dt_i + 1) * TC], ALU.mult), "v")
                    for n in range(16):
                        j = dt_i * 16 + n
                        w(v, "s", dA_val(j))
                        if j >= 3:
                            w(v, "t", idmm_val(j - 3))
                        w(v, "v", C["v"])
                        inc(v.tensor_tensor(hu_bf[:, (j % 2) * TC:(j % 2 + 1) * TC],
                                            dtx_bf[:, (dt_i % 2) * TC:(dt_i % 2 + 1) * TC],
                                            bc_sb[:, n * TC:(n + 1) * TC], ALU.mult), "v")
                        hslot = (j % 2) * 512
                        inc(v.tensor_tensor_scan(sc2[:, hslot:hslot + 512],
                                                 sc1[:, (j % 2) * 520:(j % 2) * 520 + 512],
                                                 hu_bf[:, (j % 2) * TC:(j % 2 + 1) * TC],
                                                 hst[:, cp * 256 + j:cp * 256 + j + 1],
                                                 ALU.mult, ALU.add), "v")
                        inc(v.tensor_copy(hst[:, (1 - cp) * 256 + j:(1 - cp) * 256 + j + 1],
                                          sc2[:, hslot + 511:hslot + 512]), "v")
                        inc(v.tensor_tensor(hc_bf[:, (j % 3) * TC:(j % 3 + 1) * TC],
                                            sc2[:, hslot:hslot + 512],
                                            bc_sb[:, (16 + n) * TC:(17 + n) * TC], ALU.mult), "v")
                    w(v, "t", idmm_val(dt_i * 16 + 15))
                    w(v, "v", C["v"])
                    gslot = (dt_i % 2) * 512
                    inc(v.scalar_tensor_tensor(sc3[:, gslot:gslot + 512],
                                               xs_bf[:, dt_i * TC:(dt_i + 1) * TC],
                                               dD_sb[:, dt_i:dt_i + 1],
                                               P34[dt_i % 2][:], ALU.mult, ALU.add), "v")
                    inc(v.tensor_tensor(ygh_bf[:, dt_i * TC:(dt_i + 1) * TC],
                                        sc3[:, gslot:gslot + 512],
                                        sz_bf[:, dt_i * TC:(dt_i + 1) * TC], ALU.mult), "v")
            VB(ph4_v, sn)
            assert C["v"] == v0 + NDT * 67

            def ph4_t(te):
                for dt_i in range(NDT):
                    for n in range(16):
                        j = dt_i * 16 + n
                        w(te, "v", hC_val(j))
                        if n == 0 and dt_i >= 2:
                            w(te, "v", gate_val(dt_i - 2))
                        inc(te.matmul(P34[dt_i % 2][:], idb_sb[:],
                                      hc_bf[:, (j % 3) * TC:(j % 3 + 1) * TC],
                                      start=(n == 0), stop=(n == 15)), "t")
            TB(ph4_t, sn)
            assert C["t"] == t0 + 256
            v_gate_all = gate_val(15)

            sn = snap()

            def ops_psum(o):
                if o < 4:
                    return [P1, P2, P3, P4][o][:]
                return PF[:, (o - 4) * 512:(o - 3) * 512]

            def ph5_t(te, c=c, ubase=ubase, v_gate_all=v_gate_all):
                w(te, "v", v_gate_all)
                pa_hist.clear()
                for h in range(4):
                    sem, val = unit_wait_of(ubase + 8 + h)
                    w(te, sem, val)
                    bi = (ubase + 8 + h) % 2
                    for d4 in range(4):
                        d = 4 * h + d4
                        for o in range(8):
                            ins = te.matmul(ops_psum(o),
                                            wbuf[bi][:, d4 * 1024 + o * 128:d4 * 1024 + o * 128 + 128],
                                            ygh_bf[:, d * TC:(d + 1) * TC],
                                            start=(h == 0 and d4 == 0), stop=(h == 3 and d4 == 3))
                    inc(ins, "t")
                    unit_tval[ubase + 8 + h] = C["t"]
            TB(ph5_t, sn)
            flush_units()
            t_outp = C["t"]
            qc_guard = qc_final if c == 0 else 0

            def ph5_v(v, t_outp=t_outp, qc_guard=qc_guard):
                if qc_guard:
                    w(v, "qc", qc_guard)
                w(v, "t", t_outp)
                for o in range(8):
                    inc(v.tensor_copy(ascr_bf[:, o * 512:(o + 1) * 512], ops_psum(o)), "v")
            VB(ph5_v, sn)
            v_attnevac = C["v"]

            sn = snap()
            t0, v0, s0 = C["t"], C["v"], C["s"]

            def resid1_val(s4):
                return v0 + 3 * s4 + 1

            def ph6_t(te, v_attnevac=v_attnevac):
                w(te, "v", v_attnevac)
                for s4 in range(4):
                    if s4 >= 2:
                        w(te, "v", resid1_val(s4 - 2))
                    halfb = (s4 % 2) * 2048
                    for o in range(8):
                        te.transpose(PFb[:, halfb + o * 128:halfb + (o + 1) * 128],
                                     ascr_bf[:, o * 512 + s4 * 128:o * 512 + (s4 + 1) * 128],
                                     idb_sb[:])
                    inc(te.matmul(PF[:, 512:576], idb_sb[:], idb_sb[:, 0:64],
                                  start=True, stop=True), "t")
            TB(ph6_t, sn)

            def ph6_v1(v, t0=t0):
                for s4 in range(4):
                    w(v, "t", t0 + s4 + 1)
                    halfb = (s4 % 2) * 2048
                    inc(v.tensor_tensor(tmB[:, s4 * 1024:(s4 + 1) * 1024],
                                        tmA[:, s4 * 1024:(s4 + 1) * 1024],
                                        PFb[:, halfb:halfb + 1024], ALU.add), "v")
                    inc(v.tensor_reduce(lncol[:, s4:s4 + 1],
                                        tmB[:, s4 * 1024:(s4 + 1) * 1024], AX.X, ALU.add), "v")
                    inc(v.scalar_tensor_tensor(sc3[:], tmB[:, s4 * 1024:(s4 + 1) * 1024], 1.0,
                                               tmB[:, s4 * 1024:(s4 + 1) * 1024],
                                               ALU.mult, ALU.mult,
                                               accum_out=lncol[:, 4 + s4:5 + s4]), "v")
                inc(v.tensor_tensor(lncol[:, 12:16], lncol[:, 0:4], lncol[:, 0:4], ALU.mult), "v")
                inc(v.tensor_scalar(lncol[:, 8:12], lncol[:, 0:4], 1.0 / 1024, None, ALU.mult), "v")
                inc(v.tensor_copy(sc3[:, 0:512], tmB[:, 0:512]), "v")
                inc(v.scalar_tensor_tensor(lncol[:, 16:20], lncol[:, 12:16], 1.0 / 1024,
                                           lncol[:, 4:8], ALU.mult, ALU.subtract), "v")
                inc(v.tensor_copy(sc3[:, 512:1024], tmB[:, 512:1024]), "v")
            VB(ph6_v1, sn)
            assert C["v"] == v0 + 17
            v_var1 = C["v"]

            def ph6_s(s, v_var1=v_var1):
                w(s, "v", v_var1)
                inc(s.activation(out=lncol[:, 20:24], in_=lncol[:, 16:20], func=AF.Ln,
                                 scale=-1.0 / 1024, bias=lncol[:, 28:29]), "s")
                inc(s.activation(out=lncol[:, 24:28], in_=lncol[:, 20:24], func=AF.Exp,
                                 scale=-0.5), "s")
                inc(s.activation(out=sc3[:, 512:1024], in_=tmB[:, 0:512], func=AF.Copy), "s")
            SB(ph6_s, sn)
            s_rstd1 = C["s"]

            def ph6_v2(v, s_rstd1=s_rstd1):
                w(v, "s", s_rstd1)
                for s4 in range(4):
                    inc(v.tensor_scalar(tmA[:, s4 * 1024:(s4 + 1) * 1024],
                                        tmB[:, s4 * 1024:(s4 + 1) * 1024],
                                        lncol[:, 8 + s4:9 + s4], lncol[:, 24 + s4:25 + s4],
                                        ALU.subtract, ALU.mult), "v")
            VB(ph6_v2, sn)
            v_ln1ap = C["v"]

            sn = snap()
            t0, v0 = C["t"], C["v"]

            def ph7_t(te, v_ln1ap=v_ln1ap, v0=v0):
                w(te, "v", v_ln1ap)
                for j in range(8):
                    pa_wait(te)
                    for s4 in range(4):
                        te.transpose(PA[j % 2][:, s4 * 128:(s4 + 1) * 128],
                                     tmA[:, s4 * 1024 + j * 128:s4 * 1024 + (j + 1) * 128],
                                     idf_sb[:])
                    inc(te.matmul(PF[:, 0:64], idb_sb[:], idb_sb[:, 0:64],
                                  start=True, stop=True), "t")
                    pa_hist.append(("v", v0 + j + 1))
            TB(ph7_t, sn)

            def ph7_v(v, t0=t0):
                for j in range(8):
                    w(v, "t", t0 + j + 1)
                    inc(v.tensor_copy(ascr_bf[:, j * TC:(j + 1) * TC], PA[j % 2][:]), "v")
            VB(ph7_v, sn)
            v_ln1fm = C["v"]

            for th in range(2):
                sn = snap()
                t0, s0 = C["t"], C["s"]

                def gelu_val(m, s0=s0):
                    return s0 + m + 1

                def ph8_t1(te, th=th, ubase=ubase, v_ln1fm=v_ln1fm, gelu_val=gelu_val):
                    w(te, "v", v_ln1fm)
                    if th == 1:
                        w(te, "s", marks["f2evac0"])
                    for q in range(8):
                        gu = ubase + 12 + th * 16 + q
                        sem, val = unit_wait_of(gu)
                        w(te, sem, val)
                        bi = gu % 2
                        for m4 in range(4):
                            m = 4 * q + m4
                            pa_wait(te)
                            for k in range(8):
                                ins = te.matmul(
                                    PA[m % 2][:, 0:256],
                                    wbuf[bi][:, k * 512 + m4 * 128:k * 512 + m4 * 128 + 128],
                                    ascr_bf[:, k * TC + th * 256:k * TC + th * 256 + 256],
                                    start=(k == 0), stop=(k == 7))
                            inc(ins, "t")
                            pa_hist.append(("s", gelu_val(m)))
                        unit_tval[gu] = C["t"]
                TB(ph8_t1, sn)
                flush_units()

                def ph8_s1(s, th=th, t0=t0):
                    if th == 0:
                        s.add_instruction(mybir.InstLoadActFuncSet(
                            name=nc.get_next_instruction_name(),
                            act_func_set_id=_table_id("gelu_and_others"), ins=[], outs=[]))
                    for m in range(32):
                        w(s, "t", t0 + m + 1)
                        inc(s.activation(out=ygh_bf[:, m * 256:(m + 1) * 256],
                                         in_=PA[m % 2][:, 0:256], func=AF.Gelu,
                                         scale=a1_sb[:, m:m + 1], bias=b1_sb[:, m:m + 1]), "s")
                SB(ph8_s1, sn)

                sn2 = snap()

                def fps(o):
                    if o < 4:
                        return [P1, P2, P3, P4][o][:, 0:256]
                    return PF[:, (o - 4) * 512:(o - 4) * 512 + 256]

                def ph8_t2(te, th=th, ubase=ubase, gelu_val=gelu_val, fps=fps):
                    w(te, "s", gelu_val(31))
                    for g in range(8):
                        gu = ubase + 12 + th * 16 + 8 + g
                        sem, val = unit_wait_of(gu)
                        w(te, sem, val)
                        bi = gu % 2
                        for kk in range(4):
                            hk = 4 * g + kk
                            w(te, "s", gelu_val(hk))
                            for o in range(8):
                                ins = te.matmul(
                                    fps(o),
                                    wbuf[bi][:, kk * 1024 + o * 128:kk * 1024 + o * 128 + 128],
                                    ygh_bf[:, hk * 256:(hk + 1) * 256],
                                    start=(g == 0 and kk == 0),
                                    stop=(g == 7 and kk == 3))
                        inc(ins, "t")
                        unit_tval[gu] = C["t"]
                TB(ph8_t2, sn2)
                flush_units()
                t_f2 = C["t"]

                def ph8_s2(s, th=th, t_f2=t_f2, fps=fps):
                    w(s, "t", t_f2)
                    for o in range(8):
                        inc(s.activation(out=xfm_bf[:, o * TC + th * 256:o * TC + th * 256 + 256],
                                         in_=fps(o), func=AF.Identity,
                                         scale=a2_sb[:, o:o + 1], bias=b2_sb[:, o:o + 1]), "s")
                SB(ph8_s2, sn2)
                if th == 0:
                    marks["f2evac0"] = C["s"]
            s_f2done = C["s"]

            sn = snap()
            t0, v0, s0 = C["t"], C["v"], C["s"]

            def resid2_val(s4):
                return v0 + 3 * s4 + 1

            def ph9_t(te, s_f2done=s_f2done):
                w(te, "s", s_f2done)
                for s4 in range(4):
                    if s4 >= 2:
                        w(te, "v", resid2_val(s4 - 2))
                    halfb = (s4 % 2) * 2048
                    for o in range(8):
                        te.transpose(PFb[:, halfb + o * 128:halfb + (o + 1) * 128],
                                     xfm_bf[:, o * 512 + s4 * 128:o * 512 + (s4 + 1) * 128],
                                     idb_sb[:])
                    inc(te.matmul(PF[:, 512:576], idb_sb[:], idb_sb[:, 0:64],
                                  start=True, stop=True), "t")
            TB(ph9_t, sn)
            marks["ffnT"][c] = C["t"]

            def ph9_v1(v, t0=t0):
                for s4 in range(4):
                    w(v, "t", t0 + s4 + 1)
                    halfb = (s4 % 2) * 2048
                    inc(v.tensor_tensor(tmB[:, s4 * 1024:(s4 + 1) * 1024],
                                        tmA[:, s4 * 1024:(s4 + 1) * 1024],
                                        PFb[:, halfb:halfb + 1024], ALU.add), "v")
                    inc(v.tensor_reduce(lncol[:, s4:s4 + 1],
                                        tmB[:, s4 * 1024:(s4 + 1) * 1024], AX.X, ALU.add), "v")
                    inc(v.scalar_tensor_tensor(sc3[:], tmB[:, s4 * 1024:(s4 + 1) * 1024], 1.0,
                                               tmB[:, s4 * 1024:(s4 + 1) * 1024],
                                               ALU.mult, ALU.mult,
                                               accum_out=lncol[:, 4 + s4:5 + s4]), "v")
                inc(v.tensor_tensor(lncol[:, 12:16], lncol[:, 0:4], lncol[:, 0:4], ALU.mult), "v")
                inc(v.tensor_scalar(lncol[:, 8:12], lncol[:, 0:4], 1.0 / 1024, None, ALU.mult), "v")
                inc(v.tensor_copy(sc3[:, 0:512], tmB[:, 0:512]), "v")
                inc(v.scalar_tensor_tensor(lncol[:, 16:20], lncol[:, 12:16], 1.0 / 1024,
                                           lncol[:, 4:8], ALU.mult, ALU.subtract), "v")
                inc(v.tensor_copy(sc3[:, 512:1024], tmB[:, 512:1024]), "v")
            VB(ph9_v1, sn)
            assert C["v"] == v0 + 17
            v_var2 = C["v"]

            def ph9_s(s, v_var2=v_var2):
                s.add_instruction(mybir.InstLoadActFuncSet(
                    name=nc.get_next_instruction_name(),
                    act_func_set_id=_table_id("natural_log_exp_and_others"), ins=[], outs=[]))
                w(s, "v", v_var2)
                inc(s.activation(out=lncol[:, 20:24], in_=lncol[:, 16:20], func=AF.Ln,
                                 scale=-1.0 / 1024, bias=lncol[:, 28:29]), "s")
                inc(s.activation(out=lncol[:, 24:28], in_=lncol[:, 20:24], func=AF.Exp,
                                 scale=-0.5), "s")
                inc(s.activation(out=sc3[:, 512:1024], in_=tmB[:, 0:512], func=AF.Copy), "s")
            SB(ph9_s, sn)
            s_rstd2 = C["s"]

            def ph9_v2(v, s_rstd2=s_rstd2):
                w(v, "s", s_rstd2)
                for s4 in range(4):
                    inc(v.tensor_scalar(tmA[:, s4 * 1024:(s4 + 1) * 1024],
                                        tmB[:, s4 * 1024:(s4 + 1) * 1024],
                                        lncol[:, 8 + s4:9 + s4], lncol[:, 24 + s4:25 + s4],
                                        ALU.subtract, ALU.mult), "v")
            VB(ph9_v2, sn)
            v_ln2 = C["v"]

            def ph9_sync(sync, c=c, v_ln2=v_ln2):
                w(sync, "v", v_ln2)
                inc(sync.dma_start(
                    out=out_hbm[c * TC:(c + 1) * TC, :].rearrange("(s p) d -> p s d", p=128),
                    in_=tmA[:].rearrange("p (s d) -> p s d", s=4)), "qo", 16)
            blk.sync(ph9_sync)

        assert unit_state["emitted"] == unit_state["gu"]

        def fin(sync):
            w(sync, "qo", 16 * NCH)
        blk.sync(fin)

    return nc


_NC_CACHE = None


def _get_nc():
    global _NC_CACHE
    if _NC_CACHE is None:
        _NC_CACHE = build()
    return _NC_CACHE


def host_layouts(L0_attn_in_proj_w, L0_attn_conv_w, L0_attn_conv_b, L0_attn_x_proj_w,
                 L0_attn_dt_proj_w, L0_attn_dt_proj_b, L0_attn_A_log, L0_attn_D,
                 L0_attn_out_proj_w, L0_ffn_w1, L0_ffn_b1, L0_ffn_w2, L0_ffn_b2):
    f32 = np.float32

    def lay(a):
        return np.ascontiguousarray(a.astype(f32))

    WinT = np.asarray(L0_attn_in_proj_w, dtype=f32).T
    WoutT = np.asarray(L0_attn_out_proj_w, dtype=f32).T
    w1 = np.asarray(L0_ffn_w1, dtype=f32)
    w2 = np.asarray(L0_ffn_w2, dtype=f32)
    return dict(
        winT_st=lay(WinT.reshape(8, 128, 8, 512).transpose(2, 0, 1, 3)),
        woutT_st=lay(WoutT.reshape(4, 4, 128, 1024)),
        w1T_l=lay(w1.T.reshape(8, 128, 4096)),
        w2T_l=lay(w2.T.reshape(32, 128, 1024)),
        w1n_l=lay(w1.reshape(32, 128, 1024)),
        w2n_l=lay(w2.reshape(8, 128, 4096)),
        xprojT_l=lay(np.asarray(L0_attn_x_proj_w, dtype=f32).T.reshape(16, 128, 96)),
        dtT_l=lay(np.asarray(L0_attn_dt_proj_w, dtype=f32).T),
        alog_l=lay(np.asarray(L0_attn_A_log, dtype=f32).reshape(16, 128, 16)
                   .transpose(1, 0, 2).reshape(128, 256)),
        cw_l=lay(np.asarray(L0_attn_conv_w, dtype=f32).reshape(16, 128, 4)
                 .transpose(1, 0, 2).reshape(128, 64)),
        cb_l=lay(np.asarray(L0_attn_conv_b, dtype=f32).reshape(16, 128).T),
        dD_l=lay(np.asarray(L0_attn_D, dtype=f32).reshape(16, 128).T),
        dtb_l=lay(np.asarray(L0_attn_dt_proj_b, dtype=f32).reshape(16, 128).T),
        b1_l=lay(np.asarray(L0_ffn_b1, dtype=f32).reshape(32, 128).T),
        b2_l=lay(np.asarray(L0_ffn_b2, dtype=f32).reshape(8, 128).T),
        idf_in=np.eye(128, dtype=f32),
        idb_in=np.eye(128, dtype=ml_dtypes.bfloat16),
    )


def kernel(L0_attn_x, L0_attn_in_proj_w, L0_attn_conv_w, L0_attn_conv_b,
           L0_attn_x_proj_w, L0_attn_dt_proj_w, L0_attn_dt_proj_b, L0_attn_A_log,
           L0_attn_D, L0_attn_out_proj_w, L0_ffn_w1, L0_ffn_b1, L0_ffn_w2, L0_ffn_b2):
    x = np.ascontiguousarray(np.asarray(L0_attn_x, dtype=np.float32))
    shared = host_layouts(L0_attn_in_proj_w, L0_attn_conv_w, L0_attn_conv_b,
                          L0_attn_x_proj_w, L0_attn_dt_proj_w, L0_attn_dt_proj_b,
                          L0_attn_A_log, L0_attn_D, L0_attn_out_proj_w,
                          L0_ffn_w1, L0_ffn_b1, L0_ffn_w2, L0_ffn_b2)
    in_maps = [dict(x_in=np.ascontiguousarray(x[b]), **shared) for b in range(8)]
    nc = _get_nc()
    res = run_bass_kernel_spmd(nc, in_maps, core_ids=list(range(8)))
    return np.stack([np.asarray(res.results[b]["y_out"]) for b in range(8)], axis=0)


if __name__ == "__main__":
    import sys
    if "--build" in sys.argv:
        print("build ok, instructions:", build().n_instructions())
